# version 03
# baseline (speedup 1.0000x reference)
import math
import numpy as np
import ml_dtypes
import concourse.bass as bass
import concourse.mybir as mybir
from concourse.bass_utils import run_bass_kernel_spmd

F32 = mybir.dt.float32
BF16 = mybir.dt.bfloat16
I8 = mybir.dt.int8
ALU = mybir.AluOpType
ACT = mybir.ActivationFunctionType
AXX = mybir.AxisListType.X

D = 1024
NH = 16
M2 = 512
DC = 8
ICN = 22
IFF = 2816
MAGIC = 12582912.0
INV2PI = 1.0 / (2.0 * math.pi)
TWOPI = 2.0 * math.pi
TT = 512


class Prog:

    ENG_SEMS = {"tensor": "pe", "vector": "dve", "scalar": "act",
                "sync": "sq", "gpsimd": "gq"}

    def __init__(self, nc, es):
        self.nc = nc
        self.steps = []
        self.counts = {}
        self.labels = {}
        self.sems = {}
        self.phase = "pre"
        self.bsnap = {}
        for s in ("pe", "dve", "act", "sq", "gq", "aq", "cc"):
            self.sems[s] = es.enter_context(nc.semaphore(s))
            self.counts[s] = 0

    def em(self, engine, fn, deps=(), label=None, sem=None, inc=None, pad=0):
        s = sem or self.ENG_SEMS[engine]
        if inc is None:
            inc = 16 if s in ("sq", "gq", "aq") else 1
        waits = [self.labels[d] for d in deps if d is not None]
        snap = dict(self.counts)
        if pad:
            def padded(e, fn=fn, pad=pad):
                for _ in range(pad):
                    e.engine_nop()
                return fn(e)
            fn = padded
        self.counts[s] += inc
        self.steps.append((engine, fn, waits, (s, inc), snap, self.phase, self.bsnap))
        if label is not None:
            self.labels[label] = (s, self.counts[s])

    def alias(self, new, old):
        self.labels[new] = self.labels[old]

    def barrier(self):
        self.bsnap = dict(self.counts)

    def emit(self, block, serialize=False):
        for engine in ("sync", "gpsimd", "vector", "scalar", "tensor"):
            def body(eng, engine=engine):
                seen = {}
                own = self.ENG_SEMS[engine]
                for (e, fn, waits, (sname, inc), snap, phase, bsnap) in self.steps:
                    if e != engine:
                        continue
                    ser = (serialize is True or
                           (isinstance(serialize, (set, frozenset)) and phase in serialize))
                    if ser:
                        wl = [(s2, c2) for s2, c2 in snap.items() if c2 > 0]
                    else:
                        wl = list(waits) + [(s2, c2) for s2, c2 in bsnap.items()
                                            if c2 > 0]
                    for (ws, wc) in wl:
                        if ws == own or seen.get(ws, -1) >= wc:
                            continue
                        eng.wait_ge(self.sems[ws], wc)
                        seen[ws] = wc
                    ins = fn(eng)
                    if ins is not None:
                        ins.then_inc(self.sems[sname], inc)
                if engine == "sync":
                    for s2, c2 in self.counts.items():
                        if c2 > 0 and s2 != "sq":
                            eng.wait_ge(self.sems[s2], c2)
            getattr(block, engine)(body)


def build(n_tok, rg_kv, rg_all, serialize=False, dbg=False):
    import contextlib
    es = contextlib.ExitStack()
    nc = bass.Bass("TRN2", target_bir_lowering=False, debug=False)
    NT = n_tok // TT
    TC = TT // 128
    NCH = n_tok // 128

    x_d = nc.dram_tensor("x", [n_tok, D], F32, kind="ExternalInput").ap()
    wqT_d = nc.dram_tensor("wqT", [D, D], F32, kind="ExternalInput").ap()
    wkT_d = nc.dram_tensor("wkT", [D, D], F32, kind="ExternalInput").ap()
    wvT_d = nc.dram_tensor("wvT", [D, D], F32, kind="ExternalInput").ap()
    woT_d = nc.dram_tensor("woT", [D, D], F32, kind="ExternalInput").ap()
    rfT_d = nc.dram_tensor("rfT", [64, 256], F32, kind="ExternalInput").ap()
    wg_d = nc.dram_tensor("wg", [IFF, D], F32, kind="ExternalInput").ap()
    wu_d = nc.dram_tensor("wu", [IFF, D], F32, kind="ExternalInput").ap()
    wd_d = nc.dram_tensor("wd", [D, IFF], F32, kind="ExternalInput").ap()
    clip_d = nc.dram_tensor("clip", [1, 8], F32, kind="ExternalInput").ap()
    out_d = nc.dram_tensor("out", [n_tok, D], F32, kind="ExternalOutput").ap()

    dk = "ExternalOutput" if dbg else "Internal"
    x1_sp = nc.dram_tensor("x1_sp", [n_tok, D], F32, kind=dk)
    h_sp = nc.dram_tensor("h_sp", [NCH, ICN, 128, 128], F32, kind=dk)
    kv_ci = nc.dram_tensor("kv_ci", [65, NH * M2], F32)
    kv_co = nc.dram_tensor("kv_co", [65, NH * M2], F32)
    kv_dbg = (nc.dram_tensor("kv_dbg", [65, NH * M2], F32, kind="ExternalOutput")
              if dbg else None)
    kvi_dbg = (nc.dram_tensor("kvi_dbg", [65, NH * M2], F32, kind="ExternalOutput")
               if dbg else None)
    if dbg:
        graw_dbg = nc.dram_tensor("graw_dbg", [128, TT], F32, kind="ExternalOutput")
        silu_dbg = nc.dram_tensor("silu_dbg", [128, TT], F32, kind="ExternalOutput")
        uraw_dbg = nc.dram_tensor("uraw_dbg", [128, TT], F32, kind="ExternalOutput")
        xq_dbg = nc.dram_tensor("xq_dbg", [128, D], F32, kind="ExternalOutput")
        wgl_dbg = nc.dram_tensor("wgl_dbg", [128, DC * 128], F32, kind="ExternalOutput")
        sgxs_dbg = nc.dram_tensor("sgxs_dbg", [128, ICN], F32, kind="ExternalOutput")
        qkv_dbg = nc.dram_tensor("qkv_dbg", [65, NH * TT], F32, kind="ExternalOutput")
        c_dbg = nc.dram_tensor("c_dbg", [16, 2 * TT], F32, kind="ExternalOutput")
        sq_dbg = nc.dram_tensor("sq_dbg", [16, TT], F32, kind="ExternalOutput")
        at_dbg = nc.dram_tensor("at_dbg", [128, DC * TT], F32, kind="ExternalOutput")
        cl_dbg = nc.dram_tensor("cl_dbg", [1, 32], F32, kind="ExternalOutput")
        ma_dbg = nc.dram_tensor("ma_dbg", [128, 2], F32, kind="ExternalOutput")
        c2_dbg = nc.dram_tensor("c2_dbg", [1, 32], F32, kind="ExternalOutput")
        m2_dbg = nc.dram_tensor("m2_dbg", [128, ICN], F32, kind="ExternalOutput")
        ms_dbg = nc.dram_tensor("ms_dbg", [128, ICN * 8], F32, kind="ExternalOutput")
    m1_ci = nc.dram_tensor("m1_ci", [1, 8], F32)
    m1_co = nc.dram_tensor("m1_co", [1, 8], F32)
    m2_ci = nc.dram_tensor("m2_ci", [1, 8], F32)
    m2_co = nc.dram_tensor("m2_co", [1, 8], F32)

    bfd = ml_dtypes.bfloat16
    idb_d = nc.inline_tensor(np.eye(128, dtype=bfd), name="idb").ap()
    idf_d = nc.inline_tensor(np.eye(128, dtype=np.float32), name="idf").ap()
    nself_np = np.zeros((128, DC * 16), dtype=np.float32)
    for c in range(DC):
        nself_np[0:64, c * 16 + 2 * c] = 1.0
        nself_np[64:128, c * 16 + 2 * c + 1] = 1.0
    nself_d = nc.inline_tensor(nself_np, name="nself").ap()
    hsel_np = np.zeros((16, D), dtype=bfd)
    for h in range(NH):
        hsel_np[h, h * 64:(h + 1) * 64] = 1
    hsel_d = nc.inline_tensor(hsel_np, name="hsel").ap()
    onesf_d = nc.inline_tensor(np.ones((1, 128), dtype=np.float32), name="onesf").ap()
    cst_np = np.zeros((128, 8), dtype=np.float32)
    cst_np[:, 0] = -math.log(16.0)
    cst_np[:, 1] = 1e-5
    cst_np[:, 2] = 1.0 / 1024.0
    cst_np[:, 3] = -1.0
    cst_np[:, 4] = 1.0 / 127.0
    cst_d = nc.inline_tensor(cst_np, name="cst").ap()

    p = Prog(nc, es)

    def sal(stack, name, shape, dtype):
        return stack.enter_context(nc.sbuf_tensor(name, shape, dtype))

    idb = sal(es, "idb_s", [128, 128], BF16)
    idf = sal(es, "idf_s", [128, 128], F32)
    nself = sal(es, "nself_s", [128, DC * 16], F32)
    hsel = sal(es, "hsel_s", [16, D], BF16)
    onesf = sal(es, "onesf_s", [1, 128], F32)
    cst = sal(es, "cst_s", [128, 8], F32)
    rf2 = sal(es, "rf2", [128, 256], BF16)
    cells = sal(es, "cells", [1, 32], F32)
    crep = sal(es, "crep", [128, 8], F32)
    clip_s = sal(es, "clip_s", [1, 8], F32)
    lnw = sal(es, "lnw", [128, 32], F32)
    m1a = sal(es, "m1a", [128, 2], F32)
    m1s = sal(es, "m1s", [128, 64], F32)
    m1s2 = sal(es, "m1s2", [128, 64], F32)
    m2s = sal(es, "m2s", [128, ICN * 8], F32)
    m2s2 = sal(es, "m2s2", [128, ICN * 8], F32)
    m2b = sal(es, "m2b", [128, ICN], F32)
    suxs = sal(es, "suxs", [128, ICN], F32)
    qfacc = sal(es, "qfacc", [128, ICN], F32)
    row128 = sal(es, "row128", [1, 128], F32)
    sgx = sal(es, "sgx", [128, ICN], F32)
    sux = sal(es, "sux", [128, ICN], F32)
    sgxs = sal(es, "sgxs", [128, ICN], F32)
    qfac = sal(es, "qfac", [128, ICN], F32)

    P0 = es.enter_context(nc.psum_tensor("P0", [128, 1024], F32))
    P1 = es.enter_context(nc.psum_tensor("P1", [128, 1024], F32))
    P2 = es.enter_context(nc.psum_tensor("P2", [128, 1024], F32))
    P3 = es.enter_context(nc.psum_tensor("P3", [128, 1024], F32))

    p.em("sync", lambda e: e.dma_start(out=idb[:], in_=idb_d), label="l_idb")
    p.em("sync", lambda e: e.dma_start(out=idf[:], in_=idf_d), label="l_idf")
    p.em("sync", lambda e: e.dma_start(out=nself[:], in_=nself_d), label="l_nself")
    p.em("sync", lambda e: e.dma_start(out=hsel[:], in_=hsel_d), label="l_hsel")
    p.em("sync", lambda e: e.dma_start(out=onesf[:], in_=onesf_d), label="l_ones")
    p.em("sync", lambda e: e.dma_start(out=cst[:], in_=cst_d), label="l_cst")
    p.em("sync", lambda e: e.dma_start(out=clip_s[:], in_=clip_d), label="l_clip")
    p.em("vector", lambda e: e.memset(cells[:], 0.0), label="cells_zero")

    es_w = contextlib.ExitStack()
    rff = sal(es_w, "rff", [128, 256], F32)
    wq_s = sal(es_w, "wq_s", [128, DC * D], BF16)
    wk_s = sal(es_w, "wk_s", [128, DC * D], BF16)
    wv_s = sal(es_w, "wv_s", [128, DC * D], BF16)
    wo_s = sal(es_w, "wo_s", [128, DC * D], BF16)

    p.em("sync", lambda e: e.dma_start(out=rff[0:64, :], in_=rfT_d), label="l_rff0")
    p.em("sync", lambda e: e.dma_start(out=rff[64:128, :], in_=rfT_d), label="l_rff")
    p.em("vector", lambda e: e.tensor_scalar(out=rf2[:], in0=rff[:], scalar1=INV2PI,
                                             scalar2=None, op0=ALU.mult),
         deps=["l_rff"], label="rf2_ready")

    def ld_w(name, dst, src):
        for c in range(DC):
            p.em("gpsimd", lambda e, c=c, dst=dst, src=src: e.dma_start(
                out=dst[:, c * D:(c + 1) * D], in_=src[c * 128:(c + 1) * 128, :]),
                label=name if c == DC - 1 else None)
    ld_w("l_wq", wq_s, wqT_d)
    ld_w("l_wk", wk_s, wkT_d)
    ld_w("l_wv", wv_s, wvT_d)
    ld_w("l_wo", wo_s, woT_d)

    es_ab = contextlib.ExitStack()
    xT = sal(es_ab, "xT", [128, 2 * DC * TT], BF16)
    kT = sal(es_ab, "kT", [128, DC * TT], BF16)
    k2 = sal(es_ab, "k2", [128, 2 * TT], F32)
    sk16 = sal(es_ab, "sk16", [16, 2 * TT], F32)
    phU = sal(es_ab, "phU", [128, 1024], F32)
    phK = sal(es_ab, "phK", [128, 1024], F32)
    phD = sal(es_ab, "phD", [128, 1024], BF16)
    phW = sal(es_ab, "phW", [128, 1024], BF16)

    es_a = contextlib.ExitStack()
    xb = sal(es_a, "xb", [128, 2 * D], BF16)
    skT = sal(es_a, "skT", [128, 2 * TC * 16], F32)
    vaug = sal(es_a, "vaug", [128, 2 * TC * 1040], BF16)
    kf = sal(es_a, "kf", [128, 2 * TC * M2], BF16)
    kvacc = sal(es_a, "kvacc", [65, NH * M2], F32)

    p.phase = "A"
    p.em("vector", lambda e: e.memset(kvacc[:], 0.0), label="kv_zero")

    for tile in range(NT):
        buf = tile % 2
        for tc in range(TC):
            cb = tc % 2
            gdep = None
            if tc >= 2:
                gdep = f"Axb_used_{tile}_{tc-2}"
            elif tile > 0:
                gdep = f"Axb_used_{tile-1}_{TC-2+tc}"
            p.em("gpsimd", lambda e, tc=tc, cb=cb, tile=tile: e.dma_start(
                out=xb[:, cb * D:(cb + 1) * D],
                in_=x_d[tile * TT + tc * 128: tile * TT + (tc + 1) * 128, :]),
                deps=[gdep], label=f"Axb_{tile}_{tc}")
            for dc in range(DC):
                p.em("sync", lambda e, tc=tc, dc=dc, cb=cb, buf=buf: e.dma_start_transpose(
                    xT[:, buf * DC * TT + dc * TT + tc * 128:
                          buf * DC * TT + dc * TT + (tc + 1) * 128],
                    xb[:, cb * D + dc * 128: cb * D + (dc + 1) * 128]),
                    deps=[f"Axb_{tile}_{tc}",
                          f"AxT_used_{tile-2}" if tile >= 2 else None],
                    label=f"Axt_{tile}_{tc}_{dc}")
            p.alias(f"Axb_used_{tile}_{tc}", f"Axt_{tile}_{tc}_{DC-1}")
        xt_done = f"Axt_{tile}_{TC-1}_{DC-1}"

        for dc in range(DC):
            for pc in range(DC):
                dep0 = []
                if pc == 0:
                    dep0 = [xt_done, "l_wk",
                            f"Ak2_{tile}_{dc-1}" if dc > 0 else
                            (f"Ak2_{tile-1}_{DC-1}" if tile > 0 else None),
                            f"Ak2u_{tile}_{dc-1}" if dc > 0 else
                            (f"Ask_{tile-1}" if tile > 0 else None),
                            f"Akused_{tile-1}" if tile > 0 else None]
                p.em("tensor", lambda e, dc=dc, pc=pc, buf=buf: e.matmul(
                    P0[:, 0:512], wk_s[:, pc * D + dc * 128:pc * D + (dc + 1) * 128],
                    xT[:, buf * DC * TT + pc * TT:buf * DC * TT + (pc + 1) * TT],
                    start=(pc == 0), stop=(pc == DC - 1)),
                    deps=dep0, label=f"Akmm_{tile}_{dc}" if pc == DC - 1 else None)
            p.em("scalar", lambda e, dc=dc: e.copy(
                out=kT[:, dc * TT:(dc + 1) * TT], in_=P0[:, 0:512]),
                deps=[f"Akmm_{tile}_{dc}"], label=f"Akev_{tile}_{dc}")
            p.em("scalar", lambda e, dc=dc, buf=buf: e.activation(
                out=k2[:, buf * TT:(buf + 1) * TT], in_=P0[:, 0:512], func=ACT.Square),
                deps=[f"Akmm_{tile}_{dc}"], label=f"Ak2_{tile}_{dc}")
            p.em("tensor", lambda e, dc=dc, buf=buf: e.matmul(
                P1[0:16, 0:512], nself[:, dc * 16:(dc + 1) * 16],
                k2[:, buf * TT:(buf + 1) * TT],
                start=(dc == 0), stop=(dc == DC - 1)),
                deps=[f"Ak2_{tile}_{dc}", "l_nself"] +
                     ([f"Ask_{tile-1}" if tile > 0 else None] if dc == 0 else []),
                label=f"Ak2u_{tile}_{dc}")
        p.alias(f"Anrm_{tile}", f"Ak2u_{tile}_{DC-1}")
        p.em("scalar", lambda e, buf=buf: e.activation(
            out=sk16[:, buf * TT:(buf + 1) * TT], in_=P1[0:16, 0:512], func=ACT.Exp,
            scale=-0.5, bias=cst[0:16, 0:1]),
            deps=[f"Anrm_{tile}", "l_cst"], label=f"Ask_{tile}")
        for tc in range(TC):
            p.em("tensor", lambda e, tc=tc, buf=buf: e.transpose(
                P0[:, 512:512 + 16], sk16[:, buf * TT + tc * 128: buf * TT + (tc + 1) * 128],
                idf[0:16, 0:16]),
                deps=[f"Ask_{tile}", "l_idf",
                      f"AskTe_{tile}_{tc-1}" if tc > 0 else
                      (f"AskTu_{tile-1}" if tile > 0 else None)],
                label=f"AskT_{tile}_{tc}")
            p.em("scalar", lambda e, tc=tc, buf=buf: e.copy(
                out=skT[:, buf * TC * 16 + tc * 16: buf * TC * 16 + (tc + 1) * 16],
                in_=P0[:, 512:512 + 16]),
                deps=[f"AskT_{tile}_{tc}"], label=f"AskTe_{tile}_{tc}")
        for tc in range(TC):
            ps = P2 if tc % 2 == 0 else P3
            for pc in range(DC):
                for half in range(2):
                    dep0 = []
                    if pc == 0 and half == 0:
                        dep0 = [xt_done, "l_wv", f"AskTe_{tile}_{TC-1}",
                                f"Avaug_{tile}_{tc-2}" if tc >= 2 else
                                (f"Aphiu_{tile-1}_{NH-1}" if tile > 0 else None)]
                    p.em("tensor", lambda e, pc=pc, half=half, tc=tc, buf=buf, ps=ps: e.matmul(
                        ps[:, half * 512:(half + 1) * 512],
                        xT[:, buf * DC * TT + pc * TT + tc * 128:
                              buf * DC * TT + pc * TT + (tc + 1) * 128],
                        wv_s[:, pc * D + half * 512: pc * D + (half + 1) * 512],
                        start=(pc == 0), stop=(pc == DC - 1)),
                        deps=dep0,
                        label=f"Avmm_{tile}_{tc}" if (pc == DC - 1 and half == 1) else None)
            vof = buf * TC * 1040 + tc * 1040
            p.em("vector", lambda e, tc=tc, buf=buf, vof=vof, ps=ps: e.tensor_tensor(
                out=vaug[:, vof:vof + 1040].rearrange("p (h c) -> p h c", c=65)[:, :, 0:64],
                in0=ps[:].rearrange("p (h c) -> p h c", c=64),
                in1=skT[:, buf * TC * 16 + tc * 16: buf * TC * 16 + (tc + 1) * 16]
                    .unsqueeze(2).broadcast_to([128, 16, 64]),
                op=ALU.mult),
                deps=[f"Avmm_{tile}_{tc}",
                      f"Akvused_{tile-1}" if tile > 0 else None],
                label=f"Avsc_{tile}_{tc}")
            p.em("vector", lambda e, tc=tc, buf=buf, vof=vof: e.tensor_copy(
                out=vaug[:, vof:vof + 1040].rearrange("p (h c) -> p h c", c=65)[:, :, 64:65],
                in_=skT[:, buf * TC * 16 + tc * 16: buf * TC * 16 + (tc + 1) * 16]
                    .unsqueeze(2)),
                deps=[f"Avsc_{tile}_{tc}"], label=f"Avaug_{tile}_{tc}")
        p.alias(f"AskTu_{tile}", f"Avaug_{tile}_{TC-1}")
        for h in range(NH):
            hb = h % 2
            php = P2 if hb == 0 else P3
            dc, off = h // 2, (h % 2) * 64
            for tc in range(TC):
                dep0 = []
                if tc == 0:
                    dep0 = [f"Akev_{tile}_{dc}", "rf2_ready",
                            f"Aphiu_{tile}_{h-2}" if h >= 2 else f"Avaug_{tile}_{TC-1}"]
                p.em("tensor", lambda e, tc=tc, dc=dc, off=off, php=php: e.matmul(
                    php[:, tc * 256:(tc + 1) * 256],
                    kT[:, dc * TT + tc * 128: dc * TT + (tc + 1) * 128][off:off + 64, :],
                    rf2[off:off + 64, :], start=True, stop=True),
                    deps=dep0, label=f"Apjk_{tile}_{h}" if tc == TC - 1 else None)
            p.em("scalar", lambda e, php=php: e.copy(
                out=phU[:, 0:1024], in_=php[:]),
                deps=[f"Apjk_{tile}_{h}", f"Aphis_{tile}_{h-1}" if h >= 1 else None],
                label=f"Aphiu_{tile}_{h}")
            _phi(p, "A", tile, h, hb, phU, phK, phD, phW, kf, hb, TC,
                 wdep=(f"Akvmm_{tile}_{h-2}" if h >= 2 else
                       (f"Akvmm_{tile-1}_{NH-2+h}" if tile > 0 else None)))
            for tc in range(TC):
                vof = buf * TC * 1040 + tc * 1040
                dep0 = []
                if tc == 0:
                    dep0 = [f"Aphis_{tile}_{h}", f"Avaug_{tile}_{TC-1}",
                            f"Akvev_{tile}_{h-1}" if h > 0 else
                            (f"Akvev_{tile-1}_{NH-1}" if tile > 0 else "kv_zero")]
                p.em("tensor", lambda e, tc=tc, h=h, vof=vof, hb=hb: e.matmul(
                    P1[0:65, 512:1024], vaug[:, vof + h * 65: vof + (h + 1) * 65],
                    kf[:, hb * TC * M2 + tc * M2: hb * TC * M2 + (tc + 1) * M2],
                    start=(tc == 0), stop=(tc == TC - 1)),
                    deps=dep0, label=f"Akvmm_{tile}_{h}" if tc == TC - 1 else None)
            p.em("vector", lambda e, h=h: e.tensor_tensor(
                out=kvacc[:, h * M2:(h + 1) * M2], in0=P1[0:65, 512:1024],
                in1=kvacc[:, h * M2:(h + 1) * M2], op=ALU.add),
                deps=[f"Akvmm_{tile}_{h}", "kv_zero"], label=f"Akvev_{tile}_{h}")
        p.alias(f"Akvused_{tile}", f"Akvev_{tile}_{NH-1}")
        p.alias(f"Akused_{tile}", f"Aphiu_{tile}_{NH-1}")
        p.alias(f"AxT_used_{tile}", f"Akvev_{tile}_{NH-1}")
        p.barrier()

    p.phase = "KV"
    p.em("gpsimd", lambda e: e.dma_start(out=kv_ci.ap(), in_=kvacc[:]),
         deps=[f"Akvev_{NT-1}_{NH-1}"], label="kv_to_hbm")
    p.em("gpsimd", lambda e: e.collective_compute(
        "AllReduce", ALU.add, replica_groups=rg_kv,
        ins=[kv_ci.ap().opt()], outs=[kv_co.ap().opt()]),
        deps=["kv_to_hbm"], sem="cc", inc=1, label="kv_cc")
    if dbg:
        p.em("gpsimd", lambda e: e.dma_start(out=kvi_dbg.ap(), in_=kv_ci.ap()),
             deps=["kv_to_hbm"], label="kvi_dbg")
        p.em("gpsimd", lambda e: e.dma_start(out=kv_dbg.ap(), in_=kv_co.ap()),
             deps=["kv_cc"], label="kv_dbg")

    es_a.close()
    es_b = contextlib.ExitStack()
    kvT = sal(es_b, "kvT", [128, NH * 4 * 65], BF16)
    kvld = sal(es_b, "kvld", [65, 2 * M2], F32)
    for h in range(NH):
        hb = h % 2
        p.em("gpsimd", lambda e, h=h, hb=hb: e.dma_start(
            out=kvld[:, hb * M2:(hb + 1) * M2],
            in_=kv_co.ap()[:, h * M2:(h + 1) * M2]),
            deps=["kv_cc", f"kvTt_{h-2}_3" if h >= 2 else None],
            label=f"kvld_{h}")
        for fc in range(4):
            p.em("tensor", lambda e, h=h, fc=fc, hb=hb: e.transpose(
                P0[:, 0:65], kvld[:, hb * M2 + fc * 128: hb * M2 + (fc + 1) * 128],
                idf[0:65, 0:65]),
                deps=[f"kvld_{h}", "l_idf",
                      f"kvT_{h}_{fc-1}" if fc > 0 else
                      (f"kvT_{h-1}_3" if h > 0 else None)],
                label=f"kvTt_{h}_{fc}")
            p.em("scalar", lambda e, h=h, fc=fc: e.copy(
                out=kvT[:, (h * 4 + fc) * 65:(h * 4 + fc + 1) * 65], in_=P0[:, 0:65]),
                deps=[f"kvTt_{h}_{fc}"], label=f"kvT_{h}_{fc}")
    p.alias("kvT_ready", f"kvT_{NH-1}_3")

    qf = sal(es_b, "qf", [128, 4 * TT], BF16)
    qkv = sal(es_b, "qkv", [65, NH * TT], F32)
    z16 = sal(es_b, "z16", [16, TT], F32)
    c16 = sal(es_b, "c16", [16, 2 * TT], F32)
    c16b = sal(es_b, "c16b", [16, TT], BF16)
    attT = sal(es_b, "attT", [128, DC * TT], BF16)
    xf32 = sal(es_b, "xf32", [128, 2 * D], F32)
    x1t = sal(es_b, "x1t", [128, 2 * D], F32)
    xbB = sal(es_b, "xbB", [128, 2 * D], BF16)

    p.phase = "B"
    for tile in range(NT):
        buf = tile % 2
        for tc in range(TC):
            cb = tc % 2
            if tc >= 2:
                gdep = f"Bxb_used_{tile}_{tc-2}"
            elif tile > 0:
                gdep = f"Bxb_used_{tile-1}_{TC-2+tc}"
            else:
                gdep = "kv_to_hbm"
            p.em("gpsimd", lambda e, tc=tc, cb=cb, tile=tile: e.dma_start(
                out=xbB[:, cb * D:(cb + 1) * D],
                in_=x_d[tile * TT + tc * 128: tile * TT + (tc + 1) * 128, :]),
                deps=[gdep], label=f"Bxb_{tile}_{tc}")
            for dc in range(DC):
                p.em("sync", lambda e, tc=tc, dc=dc, cb=cb, buf=buf: e.dma_start_transpose(
                    xT[:, buf * DC * TT + dc * TT + tc * 128:
                          buf * DC * TT + dc * TT + (tc + 1) * 128],
                    xbB[:, cb * D + dc * 128: cb * D + (dc + 1) * 128]),
                    deps=[f"Bxb_{tile}_{tc}",
                          f"BxT_used_{tile-2}" if tile >= 2 else
                          (f"AxT_used_{NT-1}" if tile < 2 else None)],
                    label=f"Bxt_{tile}_{tc}_{dc}")
            p.alias(f"Bxb_used_{tile}_{tc}", f"Bxt_{tile}_{tc}_{DC-1}")
        xt_done = f"Bxt_{tile}_{TC-1}_{DC-1}"
        for dc in range(DC):
            for pc in range(DC):
                dep0 = []
                if pc == 0:
                    dep0 = [xt_done, "l_wq", "kvT_ready",
                            f"Bq2_{tile}_{dc-1}" if dc > 0 else
                            (f"Bq2_{tile-1}_{DC-1}" if tile > 0 else None),
                            f"Bq2u_{tile}_{dc-1}" if dc > 0 else
                            (f"Bsq_{tile-1}" if tile > 0 else None),
                            f"Bqused_{tile-1}" if tile > 0 else None]
                p.em("tensor", lambda e, dc=dc, pc=pc, buf=buf: e.matmul(
                    P0[:, 0:512], wq_s[:, pc * D + dc * 128:pc * D + (dc + 1) * 128],
                    xT[:, buf * DC * TT + pc * TT:buf * DC * TT + (pc + 1) * TT],
                    start=(pc == 0), stop=(pc == DC - 1)),
                    deps=dep0, label=f"Bqmm_{tile}_{dc}" if pc == DC - 1 else None)
            p.em("scalar", lambda e, dc=dc: e.copy(
                out=kT[:, dc * TT:(dc + 1) * TT], in_=P0[:, 0:512]),
                deps=[f"Bqmm_{tile}_{dc}"], label=f"Bqev_{tile}_{dc}")
            p.em("scalar", lambda e, dc=dc, buf=buf: e.activation(
                out=k2[:, buf * TT:(buf + 1) * TT], in_=P0[:, 0:512], func=ACT.Square),
                deps=[f"Bqmm_{tile}_{dc}"], label=f"Bq2_{tile}_{dc}")
            p.em("tensor", lambda e, dc=dc, buf=buf: e.matmul(
                P1[0:16, 0:512], nself[:, dc * 16:(dc + 1) * 16],
                k2[:, buf * TT:(buf + 1) * TT],
                start=(dc == 0), stop=(dc == DC - 1)),
                deps=[f"Bq2_{tile}_{dc}", "l_nself"] +
                     ([f"Bsq_{tile-1}" if tile > 0 else None] if dc == 0 else []),
                label=f"Bq2u_{tile}_{dc}")
        p.em("scalar", lambda e, buf=buf: e.activation(
            out=sk16[:, buf * TT:(buf + 1) * TT], in_=P1[0:16, 0:512], func=ACT.Exp,
            scale=-0.5, bias=cst[0:16, 0:1]),
            deps=[f"Bq2u_{tile}_{DC-1}", "l_cst",
                  f"Bc16_{tile-2}" if tile >= 2 else None], label=f"Bsq_{tile}")
        for h in range(NH):
            hb = h % 2
            dc, off = h // 2, (h % 2) * 64
            for mc in range(2):
                dep0 = []
                if mc == 0:
                    dep0 = [f"Bqev_{tile}_{dc}", "rf2_ready",
                            f"Bphiu_{tile}_{h-1}" if h >= 1 else
                            (f"Bsq_{tile}")]
                p.em("tensor", lambda e, mc=mc, dc=dc, off=off: e.matmul(
                    P2[:, mc * 512:(mc + 1) * 512],
                    rf2[off:off + 64, mc * 128:(mc + 1) * 128],
                    kT[:, dc * TT:(dc + 1) * TT][off:off + 64, :],
                    start=True, stop=True),
                    deps=dep0, label=f"Bpjq_{tile}_{h}" if mc == 1 else None)
            p.em("scalar", lambda e: e.copy(out=phU[:, 0:1024], in_=P2[:]),
                 deps=[f"Bpjq_{tile}_{h}", f"Bphis_{tile}_{h-1}" if h >= 1 else None],
                 label=f"Bphiu_{tile}_{h}")
            _phi(p, "B", tile, h, 0, phU, phK, phD, phW, qf, 0, TC, q_mode=True,
                 wdep=(f"Bqkv_{tile}_{h-1}" if h >= 1 else
                       (f"Bqkv_{tile-1}_{NH-1}" if tile > 0 else None)))
            qps = P0[0:65, 512:1024] if hb == 0 else P3[0:65, 0:512]
            for fc in range(4):
                dep0 = []
                if fc == 0:
                    dep0 = [f"Bphis_{tile}_{h}",
                            f"Bqkvev_{tile}_{h-2}" if h >= 2 else
                            (f"Battn_done_{tile-1}" if tile > 0 else None)]
                p.em("tensor", lambda e, fc=fc, h=h, qps=qps: e.matmul(
                    qps, kvT[:, (h * 4 + fc) * 65:(h * 4 + fc + 1) * 65],
                    qf[:, fc * TT:(fc + 1) * TT],
                    start=(fc == 0), stop=(fc == 3)),
                    deps=dep0, label=f"Bqkv_{tile}_{h}" if fc == 3 else None)
            p.em("scalar", lambda e, h=h, qps=qps: e.copy(
                out=qkv[:, h * TT:(h + 1) * TT], in_=qps),
                deps=[f"Bqkv_{tile}_{h}",
                      f"Batt_{tile-1}_{NH-1}" if (h < 2 and tile > 0) else None],
                label=f"Bqkvev_{tile}_{h}")
        p.em("sync", lambda e: e.dma_start(out=z16[:], in_=qkv[64:65, :]),
             deps=[f"Bqkvev_{tile}_{NH-1}",
                   f"Bc16_{tile-1}" if tile > 0 else None], label=f"Bz16_{tile}")
        p.em("vector", lambda e, buf=buf: e.tensor_tensor(
            out=c16[:, 0:TT], in0=z16[:], in1=sk16[:, buf * TT:(buf + 1) * TT],
            op=ALU.mult), deps=[f"Bz16_{tile}", f"Bsq_{tile}"])
        p.em("vector", lambda e: e.tensor_scalar(
            out=c16[:, 0:TT], in0=c16[:, 0:TT], scalar1=1e-6, scalar2=None,
            op0=ALU.max))
        p.em("vector", lambda e: e.reciprocal(out=c16[:, TT:2 * TT], in_=c16[:, 0:TT]))
        p.em("vector", lambda e, buf=buf: e.tensor_tensor(
            out=c16[:, TT:2 * TT], in0=c16[:, TT:2 * TT],
            in1=sk16[:, buf * TT:(buf + 1) * TT], op=ALU.mult))
        p.em("vector", lambda e: e.tensor_copy(out=c16b[:], in_=c16[:, TT:2 * TT]),
             deps=[f"Bcrep_{tile-1}_{NH-1}" if tile > 0 else None],
             label=f"Bc16_{tile}")
        for h in range(NH):
            p.em("tensor", lambda e, h=h: e.matmul(
                P3[0:64, 512:1024], hsel[:, h * 64:(h + 1) * 64], c16b[:],
                start=True, stop=True),
                deps=[f"Bc16_{tile}", "l_hsel",
                      f"Batt_{tile}_{h-1}" if h > 0 else
                      (f"Bwo_done_{tile-1}" if tile > 0 else None)],
                label=f"Bcrep_{tile}_{h}")
            dc, off = h // 2, (h % 2) * 64
            p.em("vector", lambda e, h=h, dc=dc, off=off: e.tensor_tensor(
                out=attT[:, dc * TT:(dc + 1) * TT][off:off + 64, :],
                in0=qkv[0:64, h * TT:(h + 1) * TT], in1=P3[0:64, 512:1024],
                op=ALU.mult),
                deps=[f"Bcrep_{tile}_{h}"], label=f"Batt_{tile}_{h}")
        p.alias(f"Battn_done_{tile}", f"Batt_{tile}_{NH-1}")
        if dbg and tile == NT - 1:
            p.em("sync", lambda e: e.dma_start(out=qkv_dbg.ap(), in_=qkv[:]),
                 deps=[f"Bqkvev_{tile}_{NH-1}"], label="d_qkv")
            p.em("sync", lambda e: e.dma_start(out=c_dbg.ap(), in_=c16[:]),
                 deps=[f"Bc16_{tile}"], label="d_c")
            p.em("sync", lambda e, buf=buf: e.dma_start(
                out=sq_dbg.ap(), in_=sk16[:, buf * TT:(buf + 1) * TT]),
                 deps=[f"Bsq_{tile}"], label="d_sq")
            p.em("gpsimd", lambda e: e.dma_start(out=at_dbg.ap(), in_=attT[:]),
                 deps=[f"Battn_done_{tile}"], label="d_at")
        p.alias(f"Bqused_{tile}", f"Bpjq_{tile}_{NH-1}")
        p.alias(f"BxT_used_{tile}", f"Bqmm_{tile}_{DC-1}")
        for tc in range(TC):
            yp = P1 if tc % 2 == 0 else P2
            for dc in range(DC):
                for half in range(2):
                    dep0 = []
                    if dc == 0 and half == 0:
                        dep0 = [f"Battn_done_{tile}", "l_wo",
                                f"Bphiu_{tile}_{NH-1}",
                                f"Bres_{tile}_{tc-2}" if tc >= 2 else
                                (f"Bres_{tile-1}_{TC-2+tc}" if tile > 0 else None)]
                        if tc == 0:
                            dep0.append(f"Bsq_{tile}")
                    p.em("tensor", lambda e, dc=dc, half=half, tc=tc, yp=yp: e.matmul(
                        yp[:, half * 512:(half + 1) * 512],
                        attT[:, dc * TT + tc * 128: dc * TT + (tc + 1) * 128],
                        wo_s[:, dc * D + half * 512: dc * D + (half + 1) * 512],
                        start=(dc == 0), stop=(dc == DC - 1)),
                        deps=dep0,
                        label=f"Bwo_{tile}_{tc}" if (dc == DC - 1 and half == 1) else None)
            cb = tc % 2
            p.em("scalar", lambda e, tc=tc, cb=cb, tile=tile: e.dma_start(
                out=xf32[:, cb * D:(cb + 1) * D],
                in_=x_d[tile * TT + tc * 128: tile * TT + (tc + 1) * 128, :]),
                sem="aq",
                deps=[f"Bres_{tile}_{tc-2}" if tc >= 2 else
                      (f"Bres_{tile-1}_{TC-2+tc}" if tile > 0 else "kv_to_hbm")],
                label=f"Bxf_{tile}_{tc}")
            p.em("vector", lambda e, cb=cb, yp=yp: e.tensor_tensor(
                out=x1t[:, cb * D:(cb + 1) * D], in0=yp[:],
                in1=xf32[:, cb * D:(cb + 1) * D], op=ALU.add),
                deps=[f"Bwo_{tile}_{tc}", f"Bxf_{tile}_{tc}",
                      f"Bx1sp_{tile}_{tc-2}" if tc >= 2 else
                      (f"Bx1sp_{tile-1}_{TC-2+tc}" if tile > 0 else None)],
                label=f"Bres_{tile}_{tc}")
            _ln(p, f"Bln_{tile}_{tc}", [f"Bres_{tile}_{tc}"], x1t, cb, lnw, x1t, cb, cst, phU, wb=cb)
            p.em("scalar", lambda e, tc=tc, cb=cb, tile=tile: e.dma_start(
                out=x1_sp.ap()[tile * TT + tc * 128: tile * TT + (tc + 1) * 128, :],
                in_=x1t[:, cb * D:(cb + 1) * D]), sem="aq",
                deps=[f"Bln_{tile}_{tc}_done"], label=f"Bx1sp_{tile}_{tc}")
            p.em("vector", lambda e, cb=cb, tile=tile, tc=tc: e.reduce_max(
                out=m1s[:, tile * TC + tc: tile * TC + tc + 1],
                in_=x1t[:, cb * D:(cb + 1) * D], axis=AXX,
                apply_absolute_value=True),
                deps=[f"Bln_{tile}_{tc}_done"], label=f"Bmax_{tile}_{tc}")
        p.alias(f"Bwo_done_{tile}", f"Bwo_{tile}_{TC-1}")
        p.barrier()

    p.phase = "M1"
    p.em("scalar", lambda e: e.copy(out=m1s2[:, 0:NCH], in_=m1s[:, 0:NCH]),
         deps=[f"Bmax_{NT-1}_{TC-1}"], label="m1_cp")
    p.em("vector", lambda e: e.reduce_max(out=m1a[:, 0:1], in_=m1s2[:, 0:NCH], axis=AXX),
         deps=["m1_cp"], label="m1_strip")
    p.em("tensor", lambda e: e.transpose(P0[0:1, 0:128], m1a[:, 0:1], idf[:]),
         deps=["m1_strip", "l_idf", f"Bwo_done_{NT-1}"], label="m1_tr")
    p.em("scalar", lambda e: e.copy(out=row128[:], in_=P0[0:1, 0:128]),
         deps=["m1_tr"], label="m1_row")
    p.em("vector", lambda e: e.reduce_max(out=cells[:, 0:1], in_=row128[:], axis=AXX),
         deps=["m1_row"], label="m1_red")
    p.em("scalar", lambda e: e.dma_start(out=m1_ci.ap()[:, 0:1], in_=cells[:, 0:1]),
         sem="aq", deps=["m1_red"], label="m1_out")
    p.em("gpsimd", lambda e: e.collective_compute(
        "AllReduce", ALU.max, replica_groups=rg_all,
        ins=[m1_ci.ap().opt()], outs=[m1_co.ap().opt()]),
        deps=["m1_out"], sem="cc", inc=1, label="m1_cc")
    p.em("gpsimd", lambda e: e.dma_start(out=cells[:, 1:2], in_=m1_co.ap()[:, 0:1]),
         deps=["m1_cc"], label="m1_back")
    p.em("scalar", lambda e: e.activation(out=cells[:, 2:3], in_=clip_s[:, 0:1],
                                          func=ACT.Sigmoid),
         deps=["l_clip"], label="sigx")
    p.em("vector", lambda e: e.scalar_tensor_tensor(
        out=cells[:, 3:4], in0=cells[:, 1:2], scalar=1.0 / 127.0,
        in1=cells[:, 2:3], op0=ALU.mult, op1=ALU.mult),
        deps=["sigx", "m1_back"], label="sx_cell")
    p.em("scalar", lambda e: e.copy(out=cells[:, 5:6], in_=cells[:, 3:4]),
         deps=["sx_cell"], label="sx_c2")
    p.em("vector", lambda e: e.reciprocal(out=cells[:, 4:5], in_=cells[:, 5:6]),
         deps=["sx_c2"], label="isx_cell")
    p.em("tensor", lambda e: e.matmul(P0[:, 0:8], onesf[:], cells[:, 0:8],
                                      start=True, stop=True),
         deps=["isx_cell", "l_ones", "m1_tr"], label="crep_mm")
    p.em("scalar", lambda e: e.copy(out=crep[:], in_=P0[:, 0:8]),
         deps=["crep_mm"], label="crep1")
    if dbg:
        p.em("sync", lambda e: e.dma_start(out=cl_dbg.ap(), in_=cells[:]),
             deps=["crep1"], label="d_cl")
        p.em("sync", lambda e: e.dma_start(out=ma_dbg.ap(), in_=m1a[:]),
             deps=["crep1"], label="d_ma")

    es_b.close()
    es_ab.close()
    es_w.close()

    p.phase = "F0"
    es_f1 = contextlib.ExitStack()
    xqT = sal(es_f1, "xqT", [128, DC * n_tok], BF16)
    xq8 = sal(es_f1, "xq8", [128, D], I8)
    xqb = sal(es_f1, "xqb", [128, 2 * D], BF16)
    x1f = sal(es_f1, "x1f", [128, 2 * D], F32)
    wgl = sal(es_f1, "wgl", [128, 2 * DC * 128], BF16)
    wul = sal(es_f1, "wul", [128, 2 * DC * 128], BF16)
    wraw = sal(es_f1, "wraw", [128, 2 * 2 * D], F32)
    wq8 = sal(es_f1, "wq8", [128, D], I8)
    wqb = sal(es_f1, "wqb", [128, 2 * D], BF16)
    silug = sal(es_f1, "silug", [128, 2 * TT], F32)
    praw = sal(es_f1, "praw", [128, n_tok], F32)
    redt = sal(es_f1, "redt", [128, 4], F32)
    m2a = sal(es_f1, "m2a", [128, ICN], F32)

    for ch in range(NCH):
        cb = ch % 2
        p.em("scalar", lambda e, ch=ch, cb=cb: e.dma_start(
            out=x1f[:, cb * D:(cb + 1) * D],
            in_=x1_sp.ap()[ch * 128:(ch + 1) * 128, :]), sem="aq",
            deps=["crep1", f"F0q_{ch-2}" if ch >= 2 else None],
            label=f"F0x_{ch}")
        p.em("vector", lambda e, cb=cb: e.tensor_scalar(
            out=xq8[:], in0=x1f[:, cb * D:(cb + 1) * D],
            scalar1=crep[:, 4:5], scalar2=None, op0=ALU.mult),
            deps=[f"F0x_{ch}", "crep1"], label=f"F0i8_{ch}")
        p.em("vector", lambda e, cb=cb: e.tensor_copy(
            out=xqb[:, cb * D:(cb + 1) * D], in_=xq8[:]),
            deps=[f"F0i8_{ch}", f"F0t_{ch-2}" if ch >= 2 else None],
            label=f"F0q_{ch}")
        if dbg and ch == 0:
            p.em("vector", lambda e: e.tensor_copy(out=x1f[:, D:2 * D], in_=xqb[:, 0:D]),
                 deps=["F0q_0"], label="d_xqc")
            p.em("sync", lambda e: e.dma_start(out=xq_dbg.ap(), in_=x1f[:, D:2 * D]),
                 deps=["d_xqc"], label="d_xq")
        for dc in range(DC):
            p.em("sync", lambda e, ch=ch, dc=dc, cb=cb: e.dma_start_transpose(
                xqT[:, dc * n_tok + ch * 128: dc * n_tok + (ch + 1) * 128],
                xqb[:, cb * D + dc * 128: cb * D + (dc + 1) * 128]),
                deps=[f"F0q_{ch}"] + (["d_xq"] if (dbg and ch == 0) else []),
                label=f"F0t_{ch}_{dc}")
        p.alias(f"F0t_{ch}", f"F0t_{ch}_{DC-1}")
    p.alias("F0_done", f"F0t_{NCH-1}")

    p.phase = "F1"

    def w_jit(ic, which, wsrc, wql, scol, lbl, mmlbl, xdep):
        wb = ic % 2
        wo2 = which * D
        p.em("gpsimd", lambda e, ic=ic, wb=wb, wo2=wo2, wsrc=wsrc: e.dma_start(
            out=wraw[:, wb * 2 * D + wo2: wb * 2 * D + wo2 + D],
            in_=wsrc[ic * 128:(ic + 1) * 128, :]),
            deps=[f"{lbl}q_{ic-2}" if ic >= 2 else "crep1"],
            label=f"{lbl}l_{ic}")
        p.em("vector", lambda e, ic=ic, wb=wb, wo2=wo2, which=which: e.reduce_max(
            out=redt[:, 2 + which:3 + which],
            in_=wraw[:, wb * 2 * D + wo2: wb * 2 * D + wo2 + D],
            axis=AXX, apply_absolute_value=True),
            deps=[f"{lbl}l_{ic}", f"{lbl}s_{ic-1}" if ic >= 1 else None],
            label=f"{lbl}s0_{ic}")
        p.em("scalar", lambda e, ic=ic, scol=scol, which=which: e.activation(
            out=scol[:, ic:ic + 1], in_=redt[:, 2 + which:3 + which],
            func=ACT.Identity, scale=1.0 / 127.0),
            deps=[f"{lbl}s0_{ic}"], label=f"{lbl}s_{ic}")
        p.em("vector", lambda e, ic=ic, scol=scol, which=which: e.reciprocal(
            out=redt[:, which:which + 1], in_=scol[:, ic:ic + 1]),
            deps=[f"{lbl}s_{ic}", f"{lbl}i8_{ic-1}" if ic >= 1 else None],
            label=f"{lbl}r_{ic}")
        p.em("scalar", lambda e, wb=wb, wo2=wo2, which=which: e.activation(
            out=wq8[:], in_=wraw[:, wb * 2 * D + wo2: wb * 2 * D + wo2 + D],
            func=ACT.Identity, scale=redt[:, which:which + 1]),
            deps=[f"{lbl}r_{ic}", xdep], label=f"{lbl}i8_{ic}")
        p.em("vector", lambda e, which=which: e.tensor_copy(
            out=wqb[:, which * D:(which + 1) * D], in_=wq8[:]),
            deps=[f"{lbl}i8_{ic}", f"{lbl}t_{ic-1}" if ic >= 1 else None],
            label=f"{lbl}q_{ic}")
        for hc in range(DC):
            p.em("sync", lambda e, hc=hc, wb=wb, wql=wql, which=which: e.dma_start_transpose(
                wql[:, wb * DC * 128 + hc * 128:wb * DC * 128 + (hc + 1) * 128],
                wqb[:, which * D + hc * 128: which * D + (hc + 1) * 128]),
                deps=[f"{lbl}q_{ic}",
                      f"{mmlbl}_{ic-2}_{NT-1}" if (hc == 0 and ic >= 2) else None],
                label=f"{lbl}t_{ic}_{hc}")
        p.alias(f"{lbl}t_{ic}", f"{lbl}t_{ic}_{DC-1}")

    for ic in range(ICN):
        wb = ic % 2
        w_jit(ic, 0, wg_d, wgl, sgx, "Fg", "Fgmm",
              f"Fuq_{ic-1}" if ic >= 1 else None)
        w_jit(ic, 1, wu_d, wul, sux, "Fu", "Fumm", f"Fgq_{ic}")
        p.em("vector", lambda e, ic=ic: e.tensor_tensor(
            out=sgxs[:, ic:ic + 1], in0=sgx[:, ic:ic + 1], in1=crep[:, 3:4],
            op=ALU.mult), deps=[f"Fgs_{ic}", "crep1"], label=f"sgxs_{ic}")
        p.em("vector", lambda e, ic=ic: e.tensor_tensor(
            out=suxs[:, ic:ic + 1], in0=sux[:, ic:ic + 1], in1=crep[:, 3:4],
            op=ALU.mult), deps=[f"Fus_{ic}", "crep1"], label=f"suxs_{ic}")
        for tt in range(NT):
            gp = P0[:, (tt % 2) * 512:(tt % 2) * 512 + 512]
            up = P1[:, (tt % 2) * 512:(tt % 2) * 512 + 512]
            for hc in range(DC):
                dep0 = []
                if hc == 0:
                    dep0 = [f"Fgt_{ic}", "F0_done",
                            f"Fsilu_{ic}_{tt-2}" if tt >= 2 else
                            (f"Fsilu_{ic-1}_{NT-2+tt}" if ic > 0 else None)]
                p.em("tensor", lambda e, ic=ic, hc=hc, tt=tt, wb=wb, gp=gp: e.matmul(
                    gp, wgl[:, wb * DC * 128 + hc * 128:wb * DC * 128 + (hc + 1) * 128],
                    xqT[:, hc * n_tok + tt * TT: hc * n_tok + tt * TT + TT],
                    start=(hc == 0), stop=(hc == DC - 1)),
                    deps=dep0, label=f"Fgmm_{ic}_{tt}" if hc == DC - 1 else None)
            for hc in range(DC):
                dep0 = []
                if hc == 0:
                    dep0 = [f"Fut_{ic}",
                            f"Fpraw_{ic}_{tt-2}" if tt >= 2 else
                            (f"Fpraw_{ic-1}_{NT-2+tt}" if ic > 0 else None)]
                p.em("tensor", lambda e, ic=ic, hc=hc, tt=tt, wb=wb, up=up: e.matmul(
                    up, wul[:, wb * DC * 128 + hc * 128:wb * DC * 128 + (hc + 1) * 128],
                    xqT[:, hc * n_tok + tt * TT: hc * n_tok + tt * TT + TT],
                    start=(hc == 0), stop=(hc == DC - 1)),
                    deps=dep0, label=f"Fumm_{ic}_{tt}" if hc == DC - 1 else None)
            sb_ = (tt % 2) * TT
            if dbg and ic == 0 and tt == 0:
                p.em("vector", lambda e, gp=gp: e.tensor_copy(out=silug[:, TT:2 * TT], in_=gp),
                     deps=[f"Fgmm_0_0"], label="d_gcp")
                p.em("sync", lambda e: e.dma_start(out=graw_dbg.ap(), in_=silug[:, TT:2 * TT]),
                     deps=["d_gcp"], label="d_graw")
            p.em("scalar", lambda e, ic=ic, sb_=sb_, gp=gp: e.activation(
                out=silug[:, sb_:sb_ + TT], in_=gp, func=ACT.Silu,
                scale=sgxs[:, ic:ic + 1]),
                deps=[f"Fgmm_{ic}_{tt}", f"sgxs_{ic}",
                      f"Fpraw_{ic}_{tt-2}" if tt >= 2 else
                      (f"Fpraw_{ic-1}_{NT-2+tt}" if ic > 0 else None)] +
                     (["d_graw"] if (dbg and ic == 0 and tt == 0) else []),
                label=f"Fsilu_{ic}_{tt}")
            if dbg and ic == 0 and tt == 0:
                p.em("sync", lambda e: e.dma_start(out=silu_dbg.ap(), in_=silug[:, 0:TT]),
                     deps=["Fsilu_0_0"], label="d_silu")
                p.em("vector", lambda e, up=up: e.tensor_copy(out=praw[:, TT:2 * TT], in_=up),
                     deps=[f"Fumm_0_0"], label="d_ucp")
                p.em("sync", lambda e: e.dma_start(out=uraw_dbg.ap(), in_=praw[:, TT:2 * TT]),
                     deps=["d_ucp"], label="d_uraw")
                p.em("vector", lambda e: e.tensor_copy(out=wraw[:, 0:DC * 128], in_=wgl[:, 0:DC * 128]),
                     deps=["Fgt_0"], label="d_wglc")
                p.em("sync", lambda e: e.dma_start(out=wgl_dbg.ap(), in_=wraw[:, 0:DC * 128]),
                     deps=["d_wglc"], label="d_wgl")
                p.em("sync", lambda e: e.dma_start(out=sgxs_dbg.ap(), in_=sgxs[:]),
                     deps=["sgxs_0"], label="d_sgxs")
            p.em("vector", lambda e, tt=tt, sb_=sb_, up=up: e.tensor_tensor(
                out=praw[:, tt * TT:(tt + 1) * TT], in0=up, in1=silug[:, sb_:sb_ + TT],
                op=ALU.mult),
                deps=[f"Fsilu_{ic}_{tt}", f"Fumm_{ic}_{tt}",
                      f"Fsp_{ic-1}" if (tt == 0 and ic > 0) else None],
                label=f"Fpraw_{ic}_{tt}")
            p.em("vector", lambda e, tt=tt, ic=ic: e.reduce_max(
                out=m2s[:, ic * NT + tt: ic * NT + tt + 1],
                in_=praw[:, tt * TT:(tt + 1) * TT], axis=AXX,
                apply_absolute_value=True),
                deps=[f"Fpraw_{ic}_{tt}"], label=f"Fmax_{ic}_{tt}")
        p.em("scalar", lambda e, ic=ic: e.dma_start(
            out=h_sp.ap()[:, ic, :, :].transpose([1, 0, 2]),
            in_=praw[:].rearrange("p (c t) -> p c t", t=128)), sem="aq",
            deps=[f"Fpraw_{ic}_{NT-1}", f"Fmax_{ic}_{NT-1}"], label=f"Fsp_{ic}")
        if ic % 2 == 1:
            p.barrier()

    p.phase = "M2"
    p.em("scalar", lambda e: e.copy(out=m2s2[:, 0:ICN * NT], in_=m2s[:, 0:ICN * NT]),
         deps=[f"Fmax_{ICN-1}_{NT-1}"], label="m2_cp")
    p.em("vector", lambda e: e.reduce_max(
        out=m2a[:], in_=m2s2[:, 0:ICN * NT].rearrange("p (i t) -> p i t", t=NT),
        axis=AXX, apply_absolute_value=True),
        deps=["m2_cp"], label="m2_red0")
    p.em("scalar", lambda e: e.copy(out=m2b[:], in_=m2a[:]),
         deps=["m2_red0"], label="m2_cp2")
    p.em("vector", lambda e: e.tensor_tensor(out=m2a[:], in0=m2b[:], in1=suxs[:],
                                             op=ALU.mult),
         deps=["m2_cp2", f"suxs_{ICN-1}"], label="m2sc")
    p.em("scalar", lambda e: e.copy(out=m2b[:], in_=m2a[:]),
         deps=["m2sc"], label="m2_cp3")
    p.em("vector", lambda e: e.reduce_max(out=m1a[:, 1:2], in_=m2b[:], axis=AXX),
         deps=["m2_cp3"], label="m2_col")
    p.em("tensor", lambda e: e.transpose(P2[0:1, 0:128], m1a[:, 1:2], idf[:]),
         deps=["m2_col", "l_idf"], label="m2_tr")
    p.em("scalar", lambda e: e.copy(out=row128[:], in_=P2[0:1, 0:128]),
         deps=["m2_tr"], label="m2_row")
    p.em("vector", lambda e: e.reduce_max(out=cells[:, 8:9], in_=row128[:], axis=AXX),
         deps=["m2_row"], label="m2_red")
    p.em("scalar", lambda e: e.dma_start(out=m2_ci.ap()[:, 0:1], in_=cells[:, 8:9]),
         sem="aq", deps=["m2_red"], label="m2_out")
    p.em("gpsimd", lambda e: e.collective_compute(
        "AllReduce", ALU.max, replica_groups=rg_all,
        ins=[m2_ci.ap().opt()], outs=[m2_co.ap().opt()]),
        deps=["m2_out"], sem="cc", inc=1, label="m2_cc")
    p.em("gpsimd", lambda e: e.dma_start(out=cells[:, 9:10], in_=m2_co.ap()[:, 0:1]),
         deps=["m2_cc"], label="m2_back")
    p.em("scalar", lambda e: e.activation(out=cells[:, 10:11], in_=clip_s[:, 1:2],
                                          func=ACT.Sigmoid),
         deps=["l_clip"], label="sig2")
    p.em("vector", lambda e: e.scalar_tensor_tensor(
        out=cells[:, 11:12], in0=cells[:, 9:10], scalar=1.0 / 127.0,
        in1=cells[:, 10:11], op0=ALU.mult, op1=ALU.mult),
        deps=["sig2", "m2_back"], label="s2_cell")
    p.em("scalar", lambda e: e.copy(out=cells[:, 13:14], in_=cells[:, 11:12]),
         deps=["s2_cell"], label="s2_c2")
    p.em("vector", lambda e: e.reciprocal(out=cells[:, 12:13], in_=cells[:, 13:14]),
         deps=["s2_c2"], label="is2_cell")
    p.em("tensor", lambda e: e.matmul(P2[:, 0:16], onesf[:], cells[:, 0:16],
                                      start=True, stop=True),
         deps=["is2_cell", "l_ones", "m2_tr"], label="crep_mm2")
    p.em("scalar", lambda e: e.copy(out=crep[:], in_=P2[:, 8:16]),
         deps=["crep_mm2"], label="crep2")
    p.em("vector", lambda e: e.tensor_tensor(
        out=qfac[:], in0=suxs[:], in1=crep[:, 4:5].broadcast_to([128, ICN]),
        op=ALU.mult), deps=["crep2", f"suxs_{ICN-1}"], label="qfac_v")
    p.em("scalar", lambda e: e.copy(out=qfacc[:], in_=qfac[:]),
         deps=["qfac_v"], label="qfac_ready")

    es_f1.close()
    p.phase = "F2"
    es_f2 = contextlib.ExitStack()
    wdr = sal(es_f2, "wdr", [128, ICN * 1024], BF16)
    wraw2 = sal(es_f2, "wraw2", [128, IFF], F32)
    wq82 = sal(es_f2, "wq82", [128, D], I8)
    wqb2 = sal(es_f2, "wqb2", [128, D], BF16)
    sd8 = sal(es_f2, "sd8", [128, DC], F32)
    srow = sal(es_f2, "srow", [8, 128], F32)
    row1k = sal(es_f2, "row1k", [1, 1024], F32)
    sd2r = sal(es_f2, "sd2r", [128, 1024], F32)
    hbuf = sal(es_f2, "hbuf", [128, 2 * ICN * 128], F32)
    ih8 = sal(es_f2, "ih8", [128, ICN * 128], I8)
    ihb = sal(es_f2, "ihb", [128, 2 * ICN * 128], BF16)
    y2t = sal(es_f2, "y2t", [128, 2 * D], F32)
    xr2 = sal(es_f2, "xr2", [128, 2 * D], F32)
    outt = sal(es_f2, "outt", [128, 2 * D], F32)
    rdt2 = sal(es_f2, "rdt2", [128, 2], F32)
    lnsq = sal(es_f2, "lnsq", [128, 1024], F32)

    for hc in range(DC):
        p.em("gpsimd", lambda e, hc=hc: e.dma_start(
            out=wraw2[:], in_=wd_d[hc * 128:(hc + 1) * 128, :]),
            deps=[f"Fdq_{hc-1}" if hc > 0 else "qfac_ready"], label=f"Fdl_{hc}")
        p.em("vector", lambda e, hc=hc: e.reduce_max(
            out=rdt2[:, 1:2], in_=wraw2[:], axis=AXX,
            apply_absolute_value=True),
            deps=[f"Fdl_{hc}", f"Fds_{hc-1}" if hc >= 1 else None],
            label=f"Fds0_{hc}")
        p.em("scalar", lambda e, hc=hc: e.activation(
            out=sd8[:, hc:hc + 1], in_=rdt2[:, 1:2],
            func=ACT.Identity, scale=1.0 / 127.0),
            deps=[f"Fds0_{hc}"], label=f"Fds_{hc}")
        p.em("vector", lambda e, hc=hc: e.reciprocal(out=rdt2[:, 0:1],
                                                     in_=sd8[:, hc:hc + 1]),
             deps=[f"Fds_{hc}", f"Fdi8_{hc-1}_2" if hc >= 1 else None],
             label=f"Fdr_{hc}")
        for piece in range(3):
            w0 = piece * D
            w1 = min((piece + 1) * D, IFF)
            p.em("scalar", lambda e, w0=w0, w1=w1: e.activation(
                out=wq82[:, 0:w1 - w0], in_=wraw2[:, w0:w1],
                func=ACT.Identity, scale=rdt2[:, 0:1]),
                deps=[f"Fdr_{hc}", f"Fdtp_{hc}_{piece-1}" if piece > 0 else None],
                label=f"Fdi8_{hc}_{piece}")
            p.em("vector", lambda e, w0=w0, w1=w1: e.tensor_copy(
                out=wqb2[:, 0:w1 - w0], in_=wq82[:, 0:w1 - w0]),
                deps=[f"Fdi8_{hc}_{piece}"], label=f"Fdb_{hc}_{piece}")
            lastt = None
            for ic0 in range((w1 - w0) // 128):
                ic = piece * 8 + ic0
                p.em("sync", lambda e, ic=ic, ic0=ic0, hc=hc: e.dma_start_transpose(
                    wdr[:, ic * 1024 + hc * 128: ic * 1024 + (hc + 1) * 128],
                    wqb2[:, ic0 * 128:(ic0 + 1) * 128]),
                    deps=[f"Fdb_{hc}_{piece}"], label=f"Fdt_{hc}_{ic}")
                lastt = f"Fdt_{hc}_{ic}"
            p.alias(f"Fdtp_{hc}_{piece}", lastt)
        p.alias(f"Fdq_{hc}", f"Fdt_{hc}_{ICN-1}")
    p.alias("wd_ready", f"Fdq_{DC-1}")
    p.em("tensor", lambda e: e.transpose(P2[0:8, 0:128], sd8[:], idf[:]),
         deps=[f"Fds_{DC-1}", "l_idf", "crep2"], label="sd_tr")
    p.em("scalar", lambda e: e.copy(out=srow[:], in_=P2[0:8, 0:128]),
         deps=["sd_tr"], label="sd_row8")
    p.em("sync", lambda e: e.dma_start(out=row1k[:], in_=srow[:]),
         deps=["sd_row8"], label="sd_row")
    p.em("vector", lambda e: e.tensor_tensor(
        out=row1k[:], in0=row1k[:], in1=cells[0:1, 13:14].broadcast_to([1, 1024]),
        op=ALU.mult), deps=["sd_row", "s2_c2"], label="sd_rowsc")
    for half in range(2):
        p.em("tensor", lambda e, half=half: e.matmul(
            P2[:, half * 512:(half + 1) * 512], onesf[:],
            row1k[:, half * 512:(half + 1) * 512], start=True, stop=True),
            deps=["sd_rowsc", "sd_tr"], label=f"sdrep_{half}")
    p.em("scalar", lambda e: e.copy(out=sd2r[:], in_=P2[:]),
         deps=["sdrep_0", "sdrep_1"], label="sd2r_ready")

    for ch in range(NCH):
        hb2 = ch % 2
        p.em("scalar", lambda e, ch=ch, hb2=hb2: e.dma_start(
            out=hbuf[:, hb2 * ICN * 128:(hb2 + 1) * ICN * 128],
            in_=h_sp.ap()[ch].transpose([1, 0, 2])), sem="aq",
            deps=["qfac_ready", f"F2q_{ch-2}" if ch >= 2 else None],
            label=f"F2h_{ch}")
        p.em("vector", lambda e, hb2=hb2: e.tensor_tensor(
            out=ih8[:].rearrange("p (c t) -> p c t", t=128),
            in0=hbuf[:, hb2 * ICN * 128:(hb2 + 1) * ICN * 128]
                .rearrange("p (c t) -> p c t", t=128),
            in1=qfacc[:].unsqueeze(2).broadcast_to([128, ICN, 128]),
            op=ALU.mult),
            deps=[f"F2h_{ch}", "qfac_ready",
                  f"F2ib_{ch-1}" if ch >= 1 else None], label=f"F2q_{ch}")
        p.em("vector", lambda e, hb2=hb2: e.tensor_copy(
            out=ihb[:, hb2 * ICN * 128:(hb2 + 1) * ICN * 128], in_=ih8[:]),
            deps=[f"F2q_{ch}", f"F2mm_{ch-2}" if ch >= 2 else None],
            label=f"F2ib_{ch}")
        yp = P0 if ch % 2 == 0 else P1
        for ic in range(ICN):
            for half in range(2):
                dep0 = []
                if ic == 0 and half == 0:
                    dep0 = [f"F2ib_{ch}", "wd_ready",
                            f"F2y2_{ch-2}" if ch >= 2 else None]
                p.em("tensor", lambda e, ic=ic, half=half, hb2=hb2, yp=yp: e.matmul(
                    yp[:, half * 512:(half + 1) * 512],
                    ihb[:, hb2 * ICN * 128 + ic * 128:hb2 * ICN * 128 + (ic + 1) * 128],
                    wdr[:, ic * 1024 + half * 512: ic * 1024 + (half + 1) * 512],
                    start=(ic == 0), stop=(ic == ICN - 1)),
                    deps=dep0,
                    label=f"F2mm_{ch}" if (ic == ICN - 1 and half == 1) else None)
        cb = ch % 2
        p.em("scalar", lambda e, ch=ch, cb=cb: e.dma_start(
            out=xr2[:, cb * D:(cb + 1) * D],
            in_=x1_sp.ap()[ch * 128:(ch + 1) * 128, :]), sem="aq",
            deps=[f"F2res_{ch-2}" if ch >= 2 else "qfac_ready"], label=f"F2x1_{ch}")
        p.em("vector", lambda e, cb=cb, yp=yp: e.tensor_tensor(
            out=y2t[:, cb * D:(cb + 1) * D], in0=yp[:], in1=sd2r[:], op=ALU.mult),
            deps=[f"F2mm_{ch}", "sd2r_ready",
                  f"F2ln_{ch-2}_done" if ch >= 2 else None], label=f"F2y2_{ch}")
        p.em("vector", lambda e, cb=cb: e.tensor_tensor(
            out=y2t[:, cb * D:(cb + 1) * D], in0=y2t[:, cb * D:(cb + 1) * D],
            in1=xr2[:, cb * D:(cb + 1) * D], op=ALU.add),
            deps=[f"F2x1_{ch}", f"F2y2_{ch}"], label=f"F2res_{ch}")
        _ln(p, f"F2ln_{ch}", [f"F2res_{ch}"], y2t, cb, lnw, outt, cb, cst, lnsq,
            adeps=[f"F2out_{ch-2}" if ch >= 2 else None], wb=cb)
        p.em("scalar", lambda e, ch=ch, cb=cb: e.dma_start(
            out=out_d[ch * 128:(ch + 1) * 128, :], in_=outt[:, cb * D:(cb + 1) * D]),
            sem="aq", deps=[f"F2ln_{ch}_done"], label=f"F2out_{ch}")
        if ch % 2 == 1:
            p.barrier()

    es_f2.close()

    with nc.Block() as block:
        p.emit(block, serialize=serialize)
    return nc


def _phi(p, ph, tile, h, hb, phU, phK, phD, phW, kfdst, dbuf, tcs, q_mode=False, wdep=None):
    uo = 0
    phM = phK.bitcast(BF16)
    lu = f"{ph}phiu_{tile}_{h}"
    p.em("vector", lambda e: e.tensor_scalar(
        out=phK[:, uo:uo + 1024], in0=phU[:, uo:uo + 1024], scalar1=MAGIC,
        scalar2=MAGIC, op0=ALU.add, op1=ALU.subtract),
        deps=[lu], label=f"{ph}phik_{tile}_{h}")
    p.em("vector", lambda e: e.tensor_sub(
        out=phD[:, uo:uo + 1024], in0=phU[:, uo:uo + 1024], in1=phK[:, uo:uo + 1024]),
        deps=[f"{ph}phik_{tile}_{h}"], label=f"{ph}phid_{tile}_{h}")
    p.em("vector", lambda e: e.tensor_scalar(
        out=phW[:, uo:uo + 1024], in0=phD[:, uo:uo + 1024], scalar1=0.25,
        scalar2=None, op0=ALU.add), deps=[f"{ph}phid_{tile}_{h}"],
        label=f"{ph}phiw0_{tile}_{h}")
    p.em("vector", lambda e: e.tensor_scalar(
        out=phM[:, 2 * uo:2 * uo + 1024], in0=phW[:, uo:uo + 1024],
        scalar1=0.5, scalar2=None, op0=ALU.is_gt),
        deps=[f"{ph}phiw0_{tile}_{h}"], label=f"{ph}phim_{tile}_{h}")
    p.em("vector", lambda e: e.tensor_sub(
        out=phW[:, uo:uo + 1024], in0=phW[:, uo:uo + 1024],
        in1=phM[:, 2 * uo:2 * uo + 1024]),
        deps=[f"{ph}phim_{tile}_{h}"], label=f"{ph}phiw_{tile}_{h}")
    if not q_mode:
        ko = dbuf * tcs * 512
        p.em("scalar", lambda e: e.activation(
            out=kfdst[:, ko:ko + tcs * 512].rearrange("p (t f) -> p t f", f=512)[:, :, 0:256],
            in_=phW[:, uo:uo + 1024].rearrange("p (t f) -> p t f", f=256),
            func=ACT.Sin, scale=TWOPI),
            deps=[f"{ph}phiw_{tile}_{h}", wdep], label=f"{ph}phic_{tile}_{h}")
        p.em("scalar", lambda e: e.activation(
            out=kfdst[:, ko:ko + tcs * 512].rearrange("p (t f) -> p t f", f=512)[:, :, 256:512],
            in_=phD[:, uo:uo + 1024].rearrange("p (t f) -> p t f", f=256),
            func=ACT.Sin, scale=TWOPI),
            deps=[f"{ph}phid_{tile}_{h}", f"{ph}phic_{tile}_{h}"],
            label=f"{ph}phis_{tile}_{h}")
    else:
        p.em("scalar", lambda e: e.activation(
            out=kfdst[:, 0:2 * TT], in_=phW[:, uo:uo + 1024],
            func=ACT.Sin, scale=TWOPI),
            deps=[f"{ph}phiw_{tile}_{h}", wdep], label=f"{ph}phic_{tile}_{h}")
        p.em("scalar", lambda e: e.activation(
            out=kfdst[:, 2 * TT:4 * TT], in_=phD[:, uo:uo + 1024],
            func=ACT.Sin, scale=TWOPI),
            deps=[f"{ph}phid_{tile}_{h}", f"{ph}phic_{tile}_{h}"],
            label=f"{ph}phis_{tile}_{h}")


def _ln(p, tag, deps0, src, cb, lnw, dst, dcb, cst, sq, adeps=(), wb=0):
    D_ = 1024
    o = wb * 16
    p.em("vector", lambda e: e.reduce_sum(out=lnw[:, o:o + 1],
                                          in_=src[:, cb * D_:(cb + 1) * D_], axis=AXX),
         deps=list(deps0), label=f"{tag}_s1")
    p.em("scalar", lambda e: e.activation(out=sq[:, 0:D_],
                                          in_=src[:, cb * D_:(cb + 1) * D_],
                                          func=ACT.Square, accum_out=lnw[:, o + 1:o + 2]),
         deps=list(deps0), label=f"{tag}_s2")
    p.em("scalar", lambda e: e.activation(out=lnw[:, o + 12:o + 13], in_=lnw[:, o:o + 1],
                                          func=ACT.Identity, scale=1.0 / 1024.0),
         deps=[f"{tag}_s1"], label=f"{tag}_mu")
    p.em("vector", lambda e: e.scalar_tensor_tensor(
        out=lnw[:, o + 2:o + 3], in0=lnw[:, o + 12:o + 13], scalar=-1.0,
        in1=lnw[:, o + 12:o + 13], op0=ALU.mult, op1=ALU.mult),
        deps=[f"{tag}_mu"], label=f"{tag}_m2")
    p.em("vector", lambda e: e.tensor_scalar(
        out=lnw[:, o + 13:o + 14], in0=lnw[:, o + 1:o + 2], scalar1=1.0 / 1024.0,
        scalar2=1e-5, op0=ALU.mult, op1=ALU.add),
        deps=[f"{tag}_s2"], label=f"{tag}_ssn")
    p.em("scalar", lambda e: e.activation(out=lnw[:, o + 14:o + 15],
                                          in_=lnw[:, o + 2:o + 3],
                                          func=ACT.Identity, bias=lnw[:, o + 13:o + 14]),
         deps=[f"{tag}_m2", f"{tag}_ssn"], label=f"{tag}_var")
    p.em("vector", lambda e: e.reciprocal(out=lnw[:, o + 3:o + 4],
                                          in_=lnw[:, o + 14:o + 15]),
         deps=[f"{tag}_var"], label=f"{tag}_v2")
    p.em("scalar", lambda e: e.activation(out=lnw[:, o + 4:o + 5],
                                          in_=lnw[:, o + 3:o + 4], func=ACT.Sqrt),
         deps=[f"{tag}_v2"], label=f"{tag}_v3")
    p.em("vector", lambda e: e.scalar_tensor_tensor(
        out=lnw[:, o + 5:o + 6], in0=lnw[:, o + 12:o + 13], scalar=-1.0,
        in1=lnw[:, o + 4:o + 5], op0=ALU.mult, op1=ALU.mult),
        deps=[f"{tag}_v3"], label=f"{tag}_b1")
    p.em("vector", lambda e: e.tensor_copy(out=lnw[:, o + 6:o + 7],
                                           in_=lnw[:, o + 4:o + 5]),
         deps=[f"{tag}_v3"], label=f"{tag}_b2")
    p.em("scalar", lambda e: e.activation(out=dst[:, dcb * D_:(dcb + 1) * D_],
                                          in_=src[:, cb * D_:(cb + 1) * D_],
                                          func=ACT.Identity, scale=lnw[:, o + 6:o + 7],
                                          bias=lnw[:, o + 5:o + 6]),
         deps=[f"{tag}_b1", f"{tag}_b2"] + list(adeps), label=f"{tag}_done")


_CACHE = {}
SERIALIZE = True
DBG = False
_LAST_RES = [None]


def kernel(L0_attn_x, L0_attn_wq, L0_attn_wk, L0_attn_wv, L0_attn_wo, L0_attn_rf,
           L0_ffn_clip_x, L0_ffn_clip_down, L0_ffn_w_gate, L0_ffn_w_up, L0_ffn_w_down):
    n_cores = 8
    x = np.ascontiguousarray(np.asarray(L0_attn_x, dtype=np.float32)).reshape(-1, D)
    n_tok = x.shape[0] // n_cores
    wqT = np.ascontiguousarray(np.asarray(L0_attn_wq, np.float32).T)
    wkT = np.ascontiguousarray(np.asarray(L0_attn_wk, np.float32).T)
    wvT = np.ascontiguousarray(np.asarray(L0_attn_wv, np.float32).T)
    woT = np.ascontiguousarray(np.asarray(L0_attn_wo, np.float32).T)
    rfT = np.ascontiguousarray(np.asarray(L0_attn_rf, np.float32).T)
    wg = np.ascontiguousarray(np.asarray(L0_ffn_w_gate, np.float32))
    wu = np.ascontiguousarray(np.asarray(L0_ffn_w_up, np.float32))
    wd = np.ascontiguousarray(np.asarray(L0_ffn_w_down, np.float32))
    clip = np.zeros((1, 8), np.float32)
    clip[0, 0] = float(np.asarray(L0_ffn_clip_x).reshape(-1)[0])
    clip[0, 1] = float(np.asarray(L0_ffn_clip_down).reshape(-1)[0])

    half = n_cores // 2
    rg_kv = [list(range(half)), list(range(half, n_cores))]
    rg_all = [list(range(n_cores))]
    key = (n_tok, repr(SERIALIZE), DBG)
    if key not in _CACHE:
        _CACHE[key] = build(n_tok, rg_kv, rg_all, serialize=SERIALIZE, dbg=DBG)
    nc = _CACHE[key]
    in_maps = []
    for c in range(n_cores):
        in_maps.append({
            "x": np.ascontiguousarray(x[c * n_tok:(c + 1) * n_tok]),
            "wqT": wqT, "wkT": wkT, "wvT": wvT, "woT": woT, "rfT": rfT,
            "wg": wg, "wu": wu, "wd": wd, "clip": clip,
        })
    res = run_bass_kernel_spmd(nc, in_maps, core_ids=list(range(n_cores)))
    _LAST_RES[0] = res
    out = np.concatenate([res.results[c]["out"] for c in range(n_cores)], axis=0)
    return out.reshape(np.asarray(L0_attn_x).shape).astype(np.float32)
